# version 01
# baseline (speedup 1.0000x reference)
import contextlib
import numpy as np
import ml_dtypes
import concourse.bass as bass
import concourse.mybir as mybir
from concourse.bass_utils import run_bass_kernel_spmd

F32 = mybir.dt.float32
BF16 = mybir.dt.bfloat16
AF = mybir.ActivationFunctionType
ALU = mybir.AluOpType
AX = mybir.AxisListType
bfdt = ml_dtypes.bfloat16

D = 1024
B = 4
S = 4096
NCORE = 8
HALF = S // 2
PCHUNK = S // 8
EPS = 1e-5

ENGINES = ("sync", "scalar", "vector", "tensor", "gpsimd")


class Prog:

    def __init__(self, nc, es):
        self.nc = nc
        self.es = es
        self.items = {e: [] for e in ENGINES}
        self.sem = {e: es.enter_context(nc.semaphore("sem_" + e)) for e in ENGINES}
        self.sem.update({e + "@dma": es.enter_context(nc.semaphore("semd_" + e))
                         for e in ENGINES})
        self.cnt = {k: 0 for k in self.sem}

    def do(self, eng, fn, deps=(), dma=False, n=1, hard=False):
        flat = []
        for d in deps:
            if d is None:
                continue
            if isinstance(d, list):
                flat.extend(x for x in d if x is not None)
            else:
                flat.append(d)
        deps = flat
        for d in deps:
            assert isinstance(d, tuple) and len(d) == 3, f"bad dep {d!r}"
        amt = 16 if dma else 1
        key = eng + "@dma" if dma else eng
        self.items[eng].append((tuple(deps), fn, amt, n, key))
        self.cnt[key] += amt * n
        return (key, self.cnt[key], dma or hard)

    def emit(self):
        block = self.es.enter_context(self.nc.Block())
        for eng in ENGINES:
            items = self.items[eng]
            if not items:
                continue

            def make_body(eng=eng, items=items):
                def body(e):
                    hw = {s: 0 for s in self.sem}
                    dmakey = eng + "@dma"
                    pending_dma = 0
                    for deps, fn, amt, n, key in items:
                        if key == dmakey and pending_dma > hw[dmakey]:
                            e.wait_ge(self.sem[dmakey], pending_dma)
                            hw[dmakey] = pending_dma
                        for (dkey, val, ddma) in deps:
                            if dkey == eng and not ddma:
                                continue
                            if val > hw[dkey]:
                                e.wait_ge(self.sem[dkey], val)
                                hw[dkey] = val
                        instrs = fn(e)
                        if instrs is None:
                            instrs = []
                        if not isinstance(instrs, (list, tuple)):
                            instrs = [instrs]
                        assert len(instrs) == n, f"{eng}: expected {n} instrs, got {len(instrs)}"
                        for ins in instrs:
                            ins.then_inc(self.sem[key], amt)
                        if key == dmakey:
                            pending_dma += amt * n
                return body

            getattr(block, eng)(make_body())
        final_counts = dict(self.cnt)

        def tail(e):
            for key2, v in final_counts.items():
                if v > 0:
                    e.wait_ge(self.sem[key2], v)
        block.gpsimd(tail)


class PsumPool:

    def __init__(self, nc, es, n=6):
        self.banks = [es.enter_context(nc.psum_tensor(f"ps{i}", [128, 512], F32))
                      for i in range(n)]
        self.last_read = [None] * n
        self.i = 0
        self.n = n

    def take(self):
        i = self.i
        self.i = (self.i + 1) % self.n
        return i, self.banks[i], self.last_read[i]

    def set_read(self, i, token):
        self.last_read[i] = token



def emit_ln_tile(P, x_ap, out_ap, tmp, deps):
    stats, mv, rstd, eps = tmp["stats"], tmp["mv"], tmp["rstd"], tmp["eps"]

    def bn_fn(e):
        e.bn_stats(stats[:, 0:6], x_ap[:, 0:512])
        return e.bn_stats(stats[:, 6:12], x_ap[:, 512:1024])
    t_bn = P.do("vector", bn_fn, deps=list(deps) + [tmp.get("last_sqrt")], hard=True)
    t_stats = P.do("vector", lambda e: e.bn_aggr(mv[:, :], stats[:, 0:12]),
                   deps=[t_bn], hard=True)

    t_sqrt = P.do("scalar", lambda e: e.activation(rstd[:, :], mv[:, 1:2], AF.Sqrt,
                                                   bias=eps[:, 0:1]),
                  deps=[t_stats, tmp.get("last_norm"), tmp.get("t_eps")])
    tmp["last_sqrt"] = t_sqrt

    t_recip = P.do("vector", lambda e: e.reciprocal(rstd[:, :], rstd[:, :]),
                   deps=[t_sqrt], hard=True)
    t_norm = P.do("vector", lambda e: e.tensor_scalar(out_ap, x_ap, mv[:, 0:1],
                                                      rstd[:, 0:1], ALU.subtract,
                                                      ALU.mult), deps=[t_recip])
    tmp["last_norm"] = t_norm
    return t_norm



def build_l1ffn():
    nc = bass.Bass("TRN2", target_bir_lowering=False, debug=False)
    x4 = nc.dram_tensor("x4", [B, PCHUNK, D], F32, kind="ExternalInput").ap()
    wT = {}
    for name in ("wf", "wi", "wc", "wo", "uf", "ui", "uc", "uo"):
        wT[name] = nc.dram_tensor(name + "T", [D, D], F32, kind="ExternalInput").ap()
    bfm = nc.dram_tensor("bfm", [128, 24], F32, kind="ExternalInput").ap()
    bo_row = nc.dram_tensor("bo_row", [1, D], F32, kind="ExternalInput").ap()
    inc_fm = nc.dram_tensor("inc_fm", [128, 8], F32, kind="ExternalInput").ap()
    ident = nc.dram_tensor("ident", [128, 128], BF16, kind="ExternalInput").ap()
    yout = nc.dram_tensor("y", [B, PCHUNK, D], F32, kind="ExternalOutput").ap()

    es = contextlib.ExitStack()
    sb = lambda name, shape, dt: es.enter_context(nc.sbuf_tensor(name, shape, dt))
    w_sb = {n: sb(n, [128, 8 * 1024], BF16) for n in ("wf", "wi", "wc", "uf", "ui", "uc")}
    lnxT = sb("lnxT", [128, 8 * 2048], BF16)
    cA = sb("cA", [128, 8 * 512], F32)
    cB = sb("cB", [128, 8 * 512], F32)
    cAb = sb("cAb", [128, 8 * 512], BF16)
    cBb = sb("cBb", [128, 8 * 512], BF16)
    gf = sb("gf", [128, 512], F32)
    gi = sb("gi", [128, 512], F32)
    gg = sb("gg", [128, 512], F32)
    xst = sb("xst", [128, 1024], F32)
    lnx_tm = sb("lnx_tm", [128, 1024], BF16)
    h_tm = sb("h_tm", [128, 1024], F32)
    norm = sb("normb", [128, 1024], F32)
    ybuf = sb("ybuf", [128, 1024], F32)
    stats = sb("stats", [128, 12], F32)
    mv = sb("mv", [128, 2], F32)
    rstd = sb("rstd", [128, 1], F32)
    eps = sb("eps", [128, 1], F32)
    ones = sb("ones", [1, 128], F32)
    zero512 = sb("zero512", [128, 512], F32)
    b_fm = sb("b_fm", [128, 24], F32)
    boro = sb("boro", [1, D], F32)
    incf = sb("incf", [128, 8], F32)
    id_sb = sb("id_sb", [128, 128], BF16)

    P = Prog(nc, es)
    pool = PsumPool(nc, es, n=6)
    trans_ps = es.enter_context(nc.psum_tensor("trans_ps", [128, 1024], BF16))
    tmp = {"stats": stats, "mv": mv, "rstd": rstd, "eps": eps}

    t_const = P.do("gpsimd", lambda e: [e.memset(ones[:, :], 1.0),
                                        e.memset(eps[:, :], EPS),
                                        e.memset(zero512[:, :], 0.0)], n=3)
    tmp["t_eps"] = t_const
    t_small = P.do("sync", lambda e: [e.dma_start(out=b_fm[:, :], in_=bfm[:, :]),
                                      e.dma_start(out=boro[:, :], in_=bo_row[:, :]),
                                      e.dma_start(out=incf[:, :], in_=inc_fm[:, :]),
                                      e.dma_start(out=id_sb[:, :], in_=ident[:, :])],
                   dma=True, n=4)

    t_w = {}
    for name in ("wf", "wi", "wc", "uf", "ui", "uc"):
        def wfn(e, name=name):
            return [e.dma_start(out=w_sb[name][:, ic * 1024:(ic + 1) * 1024],
                                in_=wT[name][ic * 128:(ic + 1) * 128, :])
                    for ic in range(8)]
        t_w[name] = P.do("gpsimd", wfn, dma=True, n=8)

    t_lnxT = [[None] * 4 for _ in range(B)]
    prev_norm = None
    prev_tr = None
    prev_ev = None
    for t in range(B):
        for p in range(4):
            t_load = P.do("sync", lambda e, t=t, p=p: e.dma_start(
                out=xst[:, :], in_=x4[t, p * 128:(p + 1) * 128, :]), dma=True,
                deps=[prev_norm])
            t_norm = emit_ln_tile(P, xst[:, :], lnx_tm[:, :], tmp,
                                  deps=[t_load, prev_tr])
            prev_norm = t_norm

            def trans_fn(e):
                ins = None
                for ic in range(8):
                    ins = e.transpose(trans_ps[:, ic * 128:(ic + 1) * 128],
                                      lnx_tm[:, ic * 128:(ic + 1) * 128],
                                      id_sb[:, :])
                return ins
            t_tr = P.do("tensor", trans_fn, deps=[t_norm, t_small, prev_ev])
            prev_tr = t_tr

            def evac_fn(e, t=t, p=p):
                off = t * 512 + p * 128
                dst = lnxT[:, :].rearrange("q (ic r) -> q ic r", ic=8)[:, :, off:off + 128]
                src = trans_ps[:, :].rearrange("q (ic r) -> q ic r", ic=8)
                return e.tensor_copy(dst, src)
            t_ev = P.do("vector", evac_fn, deps=[t_tr])
            prev_ev = t_ev
            t_lnxT[t][p] = t_ev
    t_lnxT_all = [tok for row in t_lnxT for tok in row]

    t_c0 = []
    for mc in range(8):
        def c0fn(e, mc=mc):
            e.tensor_scalar(cA[:, mc * 512:(mc + 1) * 512], zero512[:, :],
                            incf[:, mc:mc + 1], None, ALU.add)
            return e.tensor_copy(cAb[:, mc * 512:(mc + 1) * 512],
                                 cA[:, mc * 512:(mc + 1) * 512])
        t_c0.append(P.do("vector", c0fn, deps=[t_const, t_small]))

    cur, curb, nxt, nxtb = cA, cAb, cB, cBb
    t_cur = list(t_c0)
    gate_w = {"f": ("wf", "uf", 0), "i": ("wi", "ui", 8), "g": ("wc", "uc", 16)}
    last_gate_mm = None
    prev_upd = None
    for t in range(B):
        t_nxt = [None] * 8
        for mc in range(8):
            gtoks = {}
            for gname, (wname, uname, bcol) in gate_w.items():
                bi, bank, war = pool.take()

                def mmfn(e, wname=wname, uname=uname, mc=mc, t=t, bank=bank,
                         curb=curb):
                    ins = None
                    for ic in range(8):
                        ins = e.matmul(bank[:, :],
                                       w_sb[wname][:, ic * 1024 + mc * 128: ic * 1024 + mc * 128 + 128],
                                       lnxT[:, ic * 2048 + t * 512: ic * 2048 + t * 512 + 512],
                                       start=(ic == 0), stop=False)
                    for m2 in range(8):
                        ins = e.matmul(bank[:, :],
                                       w_sb[uname][:, m2 * 1024 + mc * 128: m2 * 1024 + mc * 128 + 128],
                                       curb[:, m2 * 512:(m2 + 1) * 512],
                                       start=False, stop=(m2 == 7))
                    return ins
                t_mm = P.do("tensor", mmfn,
                            deps=[t_w[wname], t_w[uname], war] + t_cur + t_lnxT_all)
                last_gate_mm = t_mm
                gdst = {"f": gf, "i": gi, "g": gg}[gname]
                func = AF.Tanh if gname == "g" else AF.Sigmoid
                t_act = P.do("scalar", lambda e, gdst=gdst, bank=bank, func=func,
                             bcol=bcol, mc=mc:
                             e.activation(gdst[:, :], bank[:, :], func,
                                          bias=b_fm[:, bcol + mc: bcol + mc + 1]),
                             deps=[t_mm, t_small, prev_upd])
                pool.set_read(bi, t_act)
                gtoks[gname] = t_act

            def updfn(e, mc=mc, cur=cur, nxt=nxt, nxtb=nxtb):
                e.tensor_tensor(norm[:, 0:512], gf[:, :], cur[:, mc * 512:(mc + 1) * 512], ALU.mult)
                e.tensor_tensor(norm[:, 512:1024], gi[:, :], gg[:, :], ALU.mult)
                e.tensor_tensor(nxt[:, mc * 512:(mc + 1) * 512],
                                norm[:, 0:512], norm[:, 512:1024], ALU.add)
                return e.tensor_copy(nxtb[:, mc * 512:(mc + 1) * 512],
                                     nxt[:, mc * 512:(mc + 1) * 512])
            t_nxt[mc] = P.do("vector", updfn,
                             deps=[gtoks["f"], gtoks["i"], gtoks["g"]])
            prev_upd = t_nxt[mc]
        cur, nxt = nxt, cur
        curb, nxtb = nxtb, curb
        t_cur = t_nxt
    c_fin_b = curb
    t_cfin = t_cur

    def wofn(e):
        return [e.dma_start(out=w_sb["wf"][:, ic * 1024:(ic + 1) * 1024],
                            in_=wT["wo"][ic * 128:(ic + 1) * 128, :]) for ic in range(8)]
    t_wo = P.do("gpsimd", wofn, dma=True, n=8, deps=[last_gate_mm])

    def uofn(e):
        return [e.dma_start(out=w_sb["wi"][:, ic * 1024:(ic + 1) * 1024],
                            in_=wT["uo"][ic * 128:(ic + 1) * 128, :]) for ic in range(8)]
    t_uo = P.do("gpsimd", uofn, dma=True, n=8, deps=[last_gate_mm])

    prev_y = None
    prev_xl = prev_norm
    prev_hnorm = None
    for t in range(B):
        for p in range(4):
            tanh_toks = []
            for half in range(2):
                bi, bank, war = pool.take()

                def hfn(e, t=t, p=p, half=half, bank=bank, c_fin_b=c_fin_b):
                    ins = None
                    for ic in range(8):
                        ins = e.matmul(
                            bank[:, :],
                            lnxT[:, ic * 2048 + t * 512 + p * 128: ic * 2048 + t * 512 + p * 128 + 128],
                            w_sb["wf"][:, ic * 1024 + half * 512: ic * 1024 + half * 512 + 512],
                            start=(ic == 0), stop=False)
                    for m2 in range(8):
                        ins = e.matmul(
                            bank[:, :],
                            c_fin_b[:, m2 * 512 + p * 128: m2 * 512 + p * 128 + 128],
                            w_sb["wi"][:, m2 * 1024 + half * 512: m2 * 1024 + half * 512 + 512],
                            start=False, stop=False)
                    ins = e.matmul(bank[:, :], ones[0:1, 0:128],
                                   boro[0:1, half * 512: half * 512 + 512],
                                   start=False, stop=True)
                    return ins
                t_h = P.do("tensor", hfn,
                           deps=[t_wo, t_uo, war, t_small, t_const] + t_cfin + [t_lnxT[t][p]])
                t_tanh = P.do("scalar", lambda e, bank=bank, half=half:
                              e.activation(h_tm[:, half * 512:(half + 1) * 512],
                                           bank[:, :], AF.Tanh),
                              deps=[t_h, prev_hnorm])
                pool.set_read(bi, t_tanh)
                tanh_toks.append(t_tanh)
            t_norm = emit_ln_tile(P, h_tm[:, :], norm[:, :], tmp,
                                  deps=tanh_toks + [prev_y])
            prev_hnorm = t_norm
            t_xl = P.do("sync", lambda e, t=t, p=p: e.dma_start(
                out=xst[:, :], in_=x4[t, p * 128:(p + 1) * 128, :]), dma=True,
                deps=[prev_xl])
            t_y = P.do("vector", lambda e: e.tensor_tensor(
                ybuf[:, :], norm[:, :], xst[:, :], ALU.add),
                deps=[t_norm, t_xl])
            prev_xl = t_y
            t_st = P.do("gpsimd", lambda e, t=t, p=p: e.dma_start(
                out=yout[t, p * 128:(p + 1) * 128, :], in_=ybuf[:, :]), dma=True,
                deps=[t_y])
            prev_y = t_st

    P.emit()
    es.close()
    return nc



def build_l0ffn_j(j):
    nc = bass.Bass("TRN2", target_bir_lowering=False, debug=False)
    x1 = nc.dram_tensor("x1", [S, D], F32, kind="ExternalInput").ap()
    wbase = nc.dram_tensor("wbase", [D, D], F32, kind="ExternalInput").ap()
    bbase = nc.dram_tensor("bbase", [1, D], F32, kind="ExternalInput").ap()
    wd1 = nc.dram_tensor("wd1", [8, D, 256], F32, kind="ExternalInput").ap()
    bd1r = nc.dram_tensor("bd1r", [1, 8 * 256], F32, kind="ExternalInput").ap()
    w2aT = nc.dram_tensor("w2aT", [8, D, 257], F32, kind="ExternalInput").ap()
    wadr = nc.dram_tensor("wadr", [8, D, 64], F32, kind="ExternalInput").ap()
    badT = nc.dram_tensor("badT", [64, 8], F32, kind="ExternalInput").ap()
    wua = nc.dram_tensor("wua", [8, 65, D], F32, kind="ExternalInput").ap()
    ident = nc.dram_tensor("ident", [128, 128], BF16, kind="ExternalInput").ap()
    yout = nc.dram_tensor("y", [HALF, D], F32, kind="ExternalOutput").ap()
    dbg_loss = nc.dram_tensor("dbg_loss", [1, 32], F32, kind="ExternalOutput").ap()

    own_tiles = (list(range(0, 8)) + list(range(24, 32))) if j == 0 else list(range(8, 24))

    es = contextlib.ExitStack()
    sb = lambda name, shape, dt: es.enter_context(nc.sbuf_tensor(name, shape, dt))
    lnxT = sb("lnxT", [128, 8 * S], BF16)
    wb_sb = sb("wb_sb", [128, 8 * 1024], BF16)
    wd1_sb = [sb(f"wd1_{pp}", [128, 8 * 256], BF16) for pp in range(2)]
    w2a_sb = [sb(f"w2a_{pp}", [128, 8 * 257], BF16) for pp in range(2)]
    wadall = sb("wadall", [128, 8 * 512], F32)
    wuaall = sb("wuaall", [65, 8 * 1024], F32)
    wdsel = sb("wdsel", [128, 512], BF16)
    wusel = sb("wusel", [65, 1024], BF16)
    badsb = sb("badsb", [64, 8], F32)
    badsel = sb("badsel", [64, 1], F32)
    h_sb = sb("h_sb", [128, 257], BF16)
    y_sb = sb("y_sb", [128, 257], BF16)
    a_sb = sb("a_sb", [65, S], BF16)
    w2g = sb("w2g", [128, 2 * 257], F32)
    w2grow = sb("w2grow", [1, 257], F32)
    gg_sb = sb("gg_sb", [128, 2 * 257], F32)
    ggrow = sb("ggrow", [1, 257], F32)
    hyc = sb("hyc", [128, 8 * 32], F32)
    scrA = sb("scrA", [128, 257], F32)
    scrB = sb("scrB", [128, 257], F32)
    scrRow = sb("scrRow", [1, 257], F32)
    scrCols = sb("scrCols", [128, 4], F32)
    lossrow = sb("lossrow", [1, 16], F32)
    ohrow = sb("ohrow", [1, 8], F32)
    ohf = sb("ohf", [128, 8], F32)
    xst = sb("xst", [128, 1024], F32)
    lnx_tm = sb("lnx_tm", [128, 1024], BF16)
    out_tm = sb("out_tm", [128, 1024], F32)
    norm = sb("normb", [128, 1024], F32)
    ybuf = sb("ybuf", [128, 1024], F32)
    stats = sb("stats", [128, 12], F32)
    mv = sb("mv", [128, 2], F32)
    rstd = sb("rstd", [128, 1], F32)
    eps = sb("eps", [128, 1], F32)
    ones = sb("ones", [1, 128], F32)
    onescol = sb("onescol", [128, 1], F32)
    negtwocol = sb("negtwocol", [128, 1], F32)
    c4096 = sb("c4096", [1, 1], F32)
    m256 = sb("m256", [1, 1], F32)
    bd1_sb = sb("bd1_sb", [1, 8 * 256], F32)
    bb_sb = sb("bb_sb", [1, D], F32)
    id_sb = sb("id_sb", [128, 128], BF16)

    P = Prog(nc, es)
    pool = PsumPool(nc, es, n=4)
    g0 = es.enter_context(nc.psum_tensor("g0", [128, 257], F32))
    g1 = es.enter_context(nc.psum_tensor("g1", [128, 257], F32))
    g2 = es.enter_context(nc.psum_tensor("g2", [1, 257], F32))
    trans_ps = es.enter_context(nc.psum_tensor("trans_ps", [128, 1024], BF16))
    tmp = {"stats": stats, "mv": mv, "rstd": rstd, "eps": eps}

    t_const = P.do("gpsimd", lambda e: [e.memset(ones[:, :], 1.0),
                                        e.memset(onescol[:, :], 1.0),
                                        e.memset(negtwocol[:, :], -2.0),
                                        e.memset(m256[:, :], 256.0),
                                        e.memset(eps[:, :], EPS),
                                        e.memset(c4096[:, :], 4096.0),
                                        e.memset(a_sb[64:65, :], 1.0)], n=7)
    tmp["t_eps"] = t_const
    t_small = P.do("sync", lambda e: [e.dma_start(out=bd1_sb[:, :], in_=bd1r[:, :]),
                                      e.dma_start(out=bb_sb[:, :], in_=bbase[:, :]),
                                      e.dma_start(out=id_sb[:, :], in_=ident[:, :]),
                                      e.dma_start(out=badsb[:, :], in_=badT[:, :])],
                   dma=True, n=4)
    t_wbase = P.do("gpsimd", lambda e: [e.dma_start(
        out=wb_sb[:, ic * 1024:(ic + 1) * 1024], in_=wbase[ic * 128:(ic + 1) * 128, :])
        for ic in range(8)], dma=True, n=8)
    t_wad = P.do("gpsimd", lambda e: [e.dma_start(
        out=wadall[:, ee * 512 + ic * 64: ee * 512 + (ic + 1) * 64],
        in_=wadr[ee, ic * 128:(ic + 1) * 128, :])
        for ee in range(8) for ic in range(8)], dma=True, n=64)
    t_wua = P.do("gpsimd", lambda e: [e.dma_start(
        out=wuaall[:, ee * 1024:(ee + 1) * 1024], in_=wua[ee, :, :])
        for ee in range(8)], dma=True, n=8)

    t_lnxT = [None] * 32
    prev_norm, prev_tr, prev_ev = None, None, None
    for i in range(32):
        t_load = P.do("sync", lambda e, i=i: e.dma_start(
            out=xst[:, :], in_=x1[i * 128:(i + 1) * 128, :]), dma=True, deps=[prev_norm])
        t_norm = emit_ln_tile(P, xst[:, :], lnx_tm[:, :], tmp, deps=[t_load, prev_tr])
        prev_norm = t_norm

        def trans_fn(e):
            ins = None
            for ic in range(8):
                ins = e.transpose(trans_ps[:, ic * 128:(ic + 1) * 128],
                                  lnx_tm[:, ic * 128:(ic + 1) * 128], id_sb[:, :])
            return ins
        t_tr = P.do("tensor", trans_fn, deps=[t_norm, t_small, prev_ev])
        prev_tr = t_tr

        def evac_fn(e, i=i):
            dst = lnxT[:, :].rearrange("q (ic r) -> q ic r", ic=8)[:, :, i * 128:(i + 1) * 128]
            src = trans_ps[:, :].rearrange("q (ic r) -> q ic r", ic=8)
            return e.tensor_copy(dst, src)
        t_ev = P.do("vector", evac_fn, deps=[t_tr])
        prev_ev = t_ev
        t_lnxT[i] = t_ev
    t_lnxT_all = list(t_lnxT)

    t_wload = [None, None]
    t_prev_wge = None
    t_lw_all = []
    prev_hread = [None]
    prev_scr = None
    for ee in range(8):
        pp = ee % 2

        def wl(e, ee=ee, pp=pp):
            return [e.dma_start(out=wd1_sb[pp][:, ic * 256:(ic + 1) * 256],
                                in_=wd1[ee, ic * 128:(ic + 1) * 128, :]) for ic in range(8)] + \
                   [e.dma_start(out=w2a_sb[pp][:, ic * 257:(ic + 1) * 257],
                                in_=w2aT[ee, ic * 128:(ic + 1) * 128, :]) for ic in range(8)]
        t_wl = P.do("gpsimd", wl, dma=True, n=16, deps=[t_wload[pp]])

        bi_w0, bw0, war0 = pool.take()
        bi_w1, bw1, war1 = pool.take()

        def wgrm_fn(e, pp=pp, bw0=bw0, bw1=bw1):
            ins = None
            for ic in range(8):
                st, sp = (ic == 0), (ic == 7)
                e.matmul(bw0[:, 0:257], w2a_sb[pp][:, ic * 257: ic * 257 + 128],
                         w2a_sb[pp][:, ic * 257:(ic + 1) * 257], start=st, stop=sp)
                e.matmul(bw1[:, 0:257], w2a_sb[pp][:, ic * 257 + 128: ic * 257 + 256],
                         w2a_sb[pp][:, ic * 257:(ic + 1) * 257], start=st, stop=sp)
                ins = e.matmul(g2[0:1, 0:257], w2a_sb[pp][:, ic * 257 + 256: ic * 257 + 257],
                               w2a_sb[pp][:, ic * 257:(ic + 1) * 257], start=st, stop=sp)
            return ins
        t_wg = P.do("tensor", wgrm_fn, deps=[t_wl, war0, war1, t_prev_wge])

        def wg_evac(e, bw0=bw0, bw1=bw1):
            e.tensor_copy(w2g[:, 0:257], bw0[:, 0:257])
            e.tensor_copy(w2g[:, 257:514], bw1[:, 0:257])
            return e.tensor_copy(w2grow[:, :], g2[0:1, 0:257])
        t_wge = P.do("vector", wg_evac, deps=[t_wg], hard=True)
        t_prev_wge = t_wge
        pool.set_read(bi_w0, t_wge)
        pool.set_read(bi_w1, t_wge)

        t_gmm_last = None
        t_hy_tokens = []
        for i in range(32):
            bi_h, bh, war_h = pool.take()
            bi_y, by, war_y = pool.take()

            def hy_fn(e, i=i, pp=pp, ee=ee, bh=bh, by=by):
                for ic in range(8):
                    lhs = lnxT[:, ic * S + i * 128: ic * S + i * 128 + 128]
                    e.matmul(bh[:, 0:256], lhs, wd1_sb[pp][:, ic * 256:(ic + 1) * 256],
                             start=(ic == 0), stop=False)
                    e.matmul(by[:, 0:257], lhs, w2a_sb[pp][:, ic * 257:(ic + 1) * 257],
                             start=(ic == 0), stop=(ic == 7))
                return e.matmul(bh[:, 0:256], ones[0:1, 0:128],
                                bd1_sb[0:1, ee * 256:(ee + 1) * 256], start=False, stop=True)
            t_hy = P.do("tensor", hy_fn,
                        deps=[t_wl, war_h, war_y, t_const, t_small] + t_lnxT_all)

            t_hrelu = P.do("scalar", lambda e, bh=bh: e.activation(
                h_sb[:, 0:256], bh[:, 0:256], AF.Relu), deps=[t_hy, prev_hread])
            pool.set_read(bi_h, t_hrelu)
            t_yev = P.do("vector", lambda e, by=by: e.tensor_copy(
                y_sb[:, 0:257], by[:, 0:257]),
                deps=[t_hy] + (prev_hread if isinstance(prev_hread, list) else [prev_hread]),
                hard=True)
            pool.set_read(bi_y, t_yev)
            t_hone = P.do("gpsimd", lambda e: e.memset(h_sb[:, 256:257], 1.0),
                          deps=[t_hrelu])

            def gmm_fn(e, i=i):
                st, sp = (i == 0), (i == 31)
                e.matmul(g0[:, 0:257], h_sb[:, 0:128], h_sb[:, 0:257], start=st, stop=sp)
                return e.matmul(g1[:, 0:257], h_sb[:, 128:256], h_sb[:, 0:257],
                                start=st, stop=sp)
            t_gmm = P.do("tensor", gmm_fn, deps=[t_hrelu, t_hone, t_yev, t_wge])
            t_gmm_last = t_gmm

            def hy_red(e, ee=ee, i=i):
                e.tensor_tensor(scrA[:, 0:257], h_sb[:, 0:257], y_sb[:, 0:257], ALU.mult)
                return e.reduce_sum(hyc[:, ee * 32 + i: ee * 32 + i + 1],
                                    scrA[:, 0:257], axis=AX.X)
            t_hyr = P.do("vector", hy_red, deps=[t_hrelu, t_hone, t_yev, prev_scr],
                         hard=True)
            t_hy_tokens.append(t_hyr)
            prev_scr = t_hyr
            prev_hread = [t_gmm, t_hyr]


        def g_evac(e):
            e.tensor_copy(gg_sb[:, 0:257], g0[:, 0:257])
            return e.tensor_copy(gg_sb[:, 257:514], g1[:, 0:257])
        t_gev = P.do("vector", g_evac, deps=[t_gmm_last], hard=True)

        bi_s, bs, war_s = pool.take()

        def dots_fn(e, ee=ee):
            e.tensor_tensor(scrA[:, 0:257], w2g[:, 0:257], gg_sb[:, 0:257], ALU.mult)
            e.tensor_tensor(scrB[:, 0:257], w2g[:, 257:514], gg_sb[:, 257:514], ALU.mult)
            e.tensor_tensor(scrA[:, 0:257], scrA[:, 0:257], scrB[:, 0:257], ALU.add)
            e.reduce_sum(scrCols[:, 0:1], scrA[:, 0:257], axis=AX.X)
            e.tensor_tensor(scrB[:, 0:1], w2g[:, 256:257], gg_sb[:, 256:257], ALU.mult)
            e.tensor_tensor(scrB[:, 1:2], w2g[:, 513:514], gg_sb[:, 513:514], ALU.mult)
            e.tensor_tensor(scrB[:, 0:1], scrB[:, 0:1], scrB[:, 1:2], ALU.add)
            e.tensor_tensor(scrCols[:, 0:1], scrCols[:, 0:1], scrB[:, 0:1], ALU.add)
            return e.reduce_sum(scrCols[:, 2:3], hyc[:, ee * 32:(ee + 1) * 32], axis=AX.X)
        t_dots = P.do("vector", dots_fn, deps=[t_gev, t_wge, prev_scr] + t_hy_tokens[-1:],
                      hard=True)
        prev_scr = t_dots

        def lsum_fn(e, bs=bs):
            e.matmul(bs[0:1, 0:1], scrCols[:, 0:1], onescol[:, 0:1], start=True, stop=False)
            e.matmul(bs[0:1, 0:1], w2grow[0:1, 256:257], c4096[0:1, 0:1],
                     start=False, stop=False)
            return e.matmul(bs[0:1, 0:1], scrCols[:, 2:3], negtwocol[:, 0:1],
                            start=False, stop=True)
        t_ls = P.do("tensor", lsum_fn, deps=[t_dots, war_s, t_const])

        def lwrite_fn(e, ee=ee, bs=bs):
            return e.tensor_copy(lossrow[:, ee:ee + 1], bs[0:1, 0:1])
        t_lw = P.do("vector", lwrite_fn, deps=[t_ls], hard=True)
        pool.set_read(bi_s, t_lw)
        t_lw_all.append(t_lw)
        t_wload[pp] = t_gmm_last

    t_min = P.do("vector", lambda e: e.tensor_reduce(
        lossrow[:, 8:9], lossrow[:, 0:8], axis=AX.X, op=ALU.min),
        deps=t_lw_all, hard=True)
    t_oh = P.do("vector", lambda e: e.tensor_scalar(
        ohrow[:, 0:8], lossrow[:, 0:8], lossrow[:, 8:9], m256[0:1, 0:1],
        ALU.subtract, ALU.is_le), deps=[t_min, t_const], hard=True)

    bi_b, bb_ps, war_b = pool.take()
    t_ohb = P.do("tensor", lambda e: e.matmul(bb_ps[:, 0:8], ones[0:1, 0:128],
                                              ohrow[0:1, 0:8], start=True, stop=True),
                 deps=[t_oh, war_b, t_const])
    t_ohe = P.do("vector", lambda e: e.tensor_copy(ohf[:, :], bb_ps[:, 0:8]),
                 deps=[t_ohb], hard=True)
    pool.set_read(bi_b, t_ohe)

    wdselB = sb("wdselB", [128, 512], BF16)
    wuselB = sb("wuselB", [65, 1024], BF16)
    badselB = sb("badselB", [64, 1], F32)

    def sel_fn(e):
        e.memset(wdselB[:, :], 0.0)
        e.memset(wuselB[:, :], 0.0)
        e.memset(badselB[:, :], 0.0)
        srcs_d = [wdselB, wdsel]
        srcs_u = [wuselB, wusel]
        srcs_b = [badselB, badsel]
        last = []
        for ee2 in range(8):
            rd, wr = ee2 % 2, (ee2 + 1) % 2
            i1 = e.scalar_tensor_tensor(srcs_d[wr][:, :], wadall[:, ee2 * 512:(ee2 + 1) * 512],
                                        ohf[:, ee2:ee2 + 1], srcs_d[rd][:, :], ALU.mult, ALU.add)
            i2 = e.scalar_tensor_tensor(srcs_u[wr][:, :], wuaall[0:65, ee2 * 1024:(ee2 + 1) * 1024],
                                        ohf[0:65, ee2:ee2 + 1], srcs_u[rd][:, :], ALU.mult, ALU.add)
            i3 = e.scalar_tensor_tensor(srcs_b[wr][:, :], badsb[:, ee2:ee2 + 1],
                                        ohf[0:64, ee2:ee2 + 1], srcs_b[rd][:, :], ALU.mult, ALU.add)
            if ee2 == 7:
                last = [i1, i2, i3]
        return last
    t_sel = P.do("vector", sel_fn, deps=[t_ohe, t_wad, t_wua, t_small], hard=True, n=3)

    P.do("gpsimd", lambda e: e.dma_start(out=dbg_loss[0:1, 0:16], in_=lossrow[:, :]),
         dma=True, deps=[t_oh])
    P.do("gpsimd", lambda e: e.dma_start(out=dbg_loss[0:1, 16:24], in_=ohrow[:, :]),
         dma=True, deps=[t_oh])

    t_arelu_all = []
    for blk in range(8):
        bi_a, ba, war_a = pool.take()

        def a_fn(e, blk=blk, ba=ba):
            ins = None
            for ic in range(8):
                ins = e.matmul(ba[0:64, 0:512], wdselB[:, ic * 64:(ic + 1) * 64],
                               lnxT[:, ic * S + blk * 512: ic * S + (blk + 1) * 512],
                               start=(ic == 0), stop=(ic == 7))
            return ins
        t_amm = P.do("tensor", a_fn, deps=[t_sel, war_a] + t_lnxT_all)
        t_ar = P.do("scalar", lambda e, blk=blk, ba=ba: e.activation(
            a_sb[0:64, blk * 512:(blk + 1) * 512], ba[0:64, 0:512], AF.Relu,
            bias=badselB[0:64, 0:1]), deps=[t_amm, t_sel])
        pool.set_read(bi_a, t_ar)
        t_arelu_all.append(t_ar)

    prev_outev = None
    prev_y = None
    prev_xl2 = prev_norm
    for k, i in enumerate(own_tiles):
        ev_toks = []
        for half in range(2):
            bi, bank, war = pool.take()

            def bafn(e, i=i, half=half, bank=bank):
                for ic in range(8):
                    e.matmul(bank[:, 0:512],
                             lnxT[:, ic * S + i * 128: ic * S + i * 128 + 128],
                             wb_sb[:, ic * 1024 + half * 512: ic * 1024 + half * 512 + 512],
                             start=(ic == 0), stop=False)
                e.matmul(bank[:, 0:512], a_sb[0:65, i * 128:(i + 1) * 128],
                         wuselB[0:65, half * 512:(half + 1) * 512], start=False, stop=False)
                return e.matmul(bank[:, 0:512], ones[0:1, 0:128],
                                bb_sb[0:1, half * 512: half * 512 + 512],
                                start=False, stop=True)
            t_ba = P.do("tensor", bafn,
                        deps=[t_wbase, t_sel, war, t_small, t_const, t_lnxT[i]] + t_arelu_all)
            t_ev = P.do("vector", lambda e, half=half, bank=bank: e.tensor_copy(
                out_tm[:, half * 512:(half + 1) * 512], bank[:, 0:512]),
                deps=[t_ba, prev_outev])
            pool.set_read(bi, t_ev)
            ev_toks.append(t_ev)
        t_normo = emit_ln_tile(P, out_tm[:, :], norm[:, :], tmp, deps=ev_toks + [prev_y])
        prev_outev = t_normo
        t_xl = P.do("sync", lambda e, i=i: e.dma_start(
            out=xst[:, :], in_=x1[i * 128:(i + 1) * 128, :]), dma=True, deps=[prev_xl2])
        t_y = P.do("vector", lambda e: e.tensor_tensor(ybuf[:, :], norm[:, :], xst[:, :],
                                                       ALU.add), deps=[t_normo, t_xl])
        prev_xl2 = t_y
        t_st = P.do("gpsimd", lambda e, k=k: e.dma_start(
            out=yout[k * 128:(k + 1) * 128, :], in_=ybuf[:, :]), dma=True, deps=[t_y])
        prev_y = t_st

    P.emit()
    es.close()
    return nc



MEXP = 8.0

def build_attn(mode, j):
    nc = bass.Bass("TRN2", target_bir_lowering=False, debug=False)
    xin = nc.dram_tensor("xin", [S, D], F32, kind="ExternalInput").ap()
    wqT = nc.dram_tensor("wqT", [D, D], F32, kind="ExternalInput").ap()
    wkT = nc.dram_tensor("wkT", [D, D], F32, kind="ExternalInput").ap()
    wvT = nc.dram_tensor("wvT", [D, D], F32, kind="ExternalInput").ap()
    woT = nc.dram_tensor("woT", [D, D], F32, kind="ExternalInput").ap()
    bqf = nc.dram_tensor("bqf", [128, 8], F32, kind="ExternalInput").ap()
    bkf = nc.dram_tensor("bkf", [128, 8], F32, kind="ExternalInput").ap()
    bvr = nc.dram_tensor("bvr", [1, D], F32, kind="ExternalInput").ap()
    bor = nc.dram_tensor("bor", [1, D], F32, kind="ExternalInput").ap()
    maskc = nc.dram_tensor("maskc", [128, 4 * 1024], BF16, kind="ExternalInput").ap()
    ident = nc.dram_tensor("ident", [128, 128], BF16, kind="ExternalInput").ap()
    yout = nc.dram_tensor("y", [HALF, D], F32, kind="ExternalOutput").ap()
    kscr = nc.dram_tensor("kscr", [128, 8 * S], BF16, kind="Internal").ap()
    vscr = nc.dram_tensor("vscr", [128, 32 * 1040], BF16, kind="Internal").ap()

    own_tiles = (list(range(0, 8)) + list(range(24, 32))) if j == 0 else list(range(8, 24))
    own_qbs = [0, 1, 6, 7] if j == 0 else [2, 3, 4, 5]

    es = contextlib.ExitStack()
    sb = lambda name, shape, dt: es.enter_context(nc.sbuf_tensor(name, shape, dt))
    lnxT = sb("lnxT", [128, 8 * S], BF16)
    wbuf = sb("wbuf", [128, 8 * 1024], BF16)
    qTo = sb("qTo", [128, 8 * HALF], BF16)
    ctxT = sb("ctxT", [128, 8 * HALF], BF16)
    kbuf = [sb(f"kbuf{i}", [128, 512], BF16) for i in range(2)]
    vbuf = [sb(f"vbuf{i}", [128, 520], BF16) for i in range(2)]
    pbuf = [sb(f"pbuf{i}", [128, 1024], BF16) for i in range(2)]
    pconst = sb("pconst", [128, 1024], BF16)
    kst = [sb(f"kst{i}", [128, 512], BF16) for i in range(2)]
    vst = [sb(f"vst{i}", [128, 1040], BF16) for i in range(2)]
    mask_sb = sb("mask_sb", [128, 4 * 1024], BF16)
    ctx_tm = sb("ctx_tm", [128, 1024], BF16)
    zrowAB = sb("zrowAB", [1, 1024], F32)
    zrow_zero = sb("zrow_zero", [1, 512], F32)
    zrec = sb("zrec", [128, 4 * 64], F32)
    one1 = sb("one1", [1, 1], F32)
    bias_m8 = sb("bias_m8", [128, 1], F32)
    bq_sb = sb("bq_sb", [128, 8], F32)
    bk_sb = sb("bk_sb", [128, 8], F32)
    bv_sb = sb("bv_sb", [1, D], F32)
    bo_sb = sb("bo_sb", [1, D], F32)
    xst = sb("xst", [128, 1024], F32)
    lnx_tm = sb("lnx_tm", [128, 1024], BF16)
    out_tm = sb("out_tm", [128, 1024], F32)
    norm = sb("normb", [128, 1024], F32)
    ybuf = sb("ybuf", [128, 1024], F32)
    stats = sb("stats", [128, 12], F32)
    mv = sb("mv", [128, 2], F32)
    rstd = sb("rstd", [128, 1], F32)
    eps = sb("eps", [128, 1], F32)
    ones = sb("ones", [1, 128], F32)
    id_sb = sb("id_sb", [128, 128], BF16)

    P = Prog(nc, es)
    pool = PsumPool(nc, es, n=3)
    SP = es.enter_context(nc.psum_tensor("SP", [128, 1024], F32))
    ctxA = es.enter_context(nc.psum_tensor("ctxA", [128, 512], F32))
    ctxB = es.enter_context(nc.psum_tensor("ctxB", [128, 512], F32))
    trans_ps = es.enter_context(nc.psum_tensor("trans_ps", [128, 1024], BF16))
    tmp = {"stats": stats, "mv": mv, "rstd": rstd, "eps": eps}

    t_const = P.do("gpsimd", lambda e: [e.memset(ones[:, :], 1.0),
                                        e.memset(eps[:, :], EPS),
                                        e.memset(one1[:, :], 1.0),
                                        e.memset(bias_m8[:, :], -MEXP),
                                        e.memset(pconst[:, :],
                                                 float(np.exp(-MEXP)) if mode == "l0" else 0.0),
                                        e.memset(vst[0][:, :], 0.0),
                                        e.memset(vst[1][:, :], 0.0),
                                        e.memset(vst[0][:, :].rearrange(
                                            "p (h c) -> p h c", c=65)[:, :, 64:65], 1.0),
                                        e.memset(vst[1][:, :].rearrange(
                                            "p (h c) -> p h c", c=65)[:, :, 64:65], 1.0),
                                        e.memset(zrow_zero[:, :], 0.0)], n=10)
    tmp["t_eps"] = t_const
    t_small = P.do("sync", lambda e: [e.dma_start(out=bq_sb[:, :], in_=bqf[:, :]),
                                      e.dma_start(out=bk_sb[:, :], in_=bkf[:, :]),
                                      e.dma_start(out=bv_sb[:, :], in_=bvr[:, :]),
                                      e.dma_start(out=bo_sb[:, :], in_=bor[:, :]),
                                      e.dma_start(out=id_sb[:, :], in_=ident[:, :]),
                                      e.dma_start(out=mask_sb[:, :], in_=maskc[:, :])],
                   dma=True, n=6)

    t_lnxT = [None] * 32
    prev_norm, prev_tr, prev_ev = None, None, None
    for i in range(32):
        t_load = P.do("sync", lambda e, i=i: e.dma_start(
            out=xst[:, :], in_=xin[i * 128:(i + 1) * 128, :]), dma=True, deps=[prev_norm])
        t_norm = emit_ln_tile(P, xst[:, :], lnx_tm[:, :], tmp, deps=[t_load, prev_tr])
        prev_norm = t_norm

        def trans_fn(e):
            ins = None
            for ic in range(8):
                ins = e.transpose(trans_ps[:, ic * 128:(ic + 1) * 128],
                                  lnx_tm[:, ic * 128:(ic + 1) * 128], id_sb[:, :])
            return ins
        t_tr = P.do("tensor", trans_fn, deps=[t_norm, t_small, prev_ev])
        prev_tr = t_tr

        def evac_fn(e, i=i):
            dst = lnxT[:, :].rearrange("q (ic r) -> q ic r", ic=8)[:, :, i * 128:(i + 1) * 128]
            srcv = trans_ps[:, :].rearrange("q (ic r) -> q ic r", ic=8)
            return e.tensor_copy(dst, srcv)
        t_ev = P.do("vector", evac_fn, deps=[t_tr])
        prev_ev = t_ev
        t_lnxT[i] = t_ev
    t_lnxT_all = list(t_lnxT)

    def wload(dram):
        def fn(e):
            return [e.dma_start(out=wbuf[:, ic * 1024:(ic + 1) * 1024],
                                in_=dram[ic * 128:(ic + 1) * 128, :]) for ic in range(8)]
        return fn

    t_wk = P.do("gpsimd", wload(wkT), dma=True, n=8)
    t_kout = []
    st_par = 0
    t_kst_free = [None, None]
    for g in range(8):
        for tg in range(8):
            bi, bank, war = pool.take()

            def kmm(e, g=g, tg=tg, bank=bank):
                ins = None
                for ic in range(8):
                    ins = e.matmul(bank[:, 0:512],
                                   wbuf[:, ic * 1024 + g * 128: ic * 1024 + g * 128 + 128],
                                   lnxT[:, ic * S + tg * 512: ic * S + (tg + 1) * 512],
                                   start=(ic == 0), stop=(ic == 7))
                return ins
            t_mm = P.do("tensor", kmm, deps=[t_wk, war] + t_lnxT_all)
            pp = st_par % 2
            st_par += 1
            t_ev2 = P.do("vector", lambda e, g=g, bank=bank, pp=pp: e.tensor_scalar(
                kst[pp][:, :], bank[:, 0:512], bk_sb[:, g:g + 1], None, ALU.add),
                deps=[t_mm, t_small, t_kst_free[pp]], hard=True)
            pool.set_read(bi, t_ev2)
            t_do = P.do("gpsimd", lambda e, g=g, tg=tg, pp=pp: e.dma_start(
                out=kscr[0:128, g * S + tg * 512: g * S + (tg + 1) * 512],
                in_=kst[pp][:, :]), dma=True, deps=[t_ev2])
            t_kst_free[pp] = t_do
            t_kout.append(t_do)

    t_wv = P.do("gpsimd", wload(wvT), dma=True, n=8, deps=t_kout[-2:])
    t_vout = []
    t_vst_free = [None, None]
    st_par = 0
    for i in range(32):
        bi0, b0, war0 = pool.take()
        bi1, b1, war1 = pool.take()

        def vmm(e, i=i, b0=b0, b1=b1):
            for ic in range(8):
                lhs = lnxT[:, ic * S + i * 128: ic * S + i * 128 + 128]
                e.matmul(b0[:, 0:512], lhs, wbuf[:, ic * 1024: ic * 1024 + 512],
                         start=(ic == 0), stop=False)
                e.matmul(b1[:, 0:512], lhs, wbuf[:, ic * 1024 + 512: ic * 1024 + 1024],
                         start=(ic == 0), stop=False)
            e.matmul(b0[:, 0:512], ones[0:1, 0:128], bv_sb[0:1, 0:512],
                     start=False, stop=True)
            return e.matmul(b1[:, 0:512], ones[0:1, 0:128], bv_sb[0:1, 512:1024],
                            start=False, stop=True)
        t_mm = P.do("tensor", vmm, deps=[t_wv, war0, war1, t_const, t_small] + t_lnxT_all)
        pp = st_par % 2
        st_par += 1

        def vev(e, pp=pp, b0=b0, b1=b1):
            dst = vst[pp][:, :].rearrange("p (h c) -> p h c", c=65)
            e.tensor_copy(dst[:, 0:8, 0:64],
                          b0[:, 0:512].rearrange("p (h c) -> p h c", c=64))
            return e.tensor_copy(dst[:, 8:16, 0:64],
                                 b1[:, 0:512].rearrange("p (h c) -> p h c", c=64))
        t_ev2 = P.do("vector", vev, deps=[t_mm, t_vst_free[pp], t_const])
        pool.set_read(bi0, t_ev2)
        pool.set_read(bi1, t_ev2)
        t_do = P.do("gpsimd", lambda e, i=i, pp=pp: e.dma_start(
            out=vscr[0:128, i * 1040:(i + 1) * 1040], in_=vst[pp][:, :]), dma=True,
            deps=[t_ev2])
        t_vst_free[pp] = t_do
        t_vout.append(t_do)

    t_wq = P.do("gpsimd", wload(wqT), dma=True, n=8, deps=t_vout[-2:])
    t_q = []
    for g in range(8):
        for oq in range(4):
            qb = own_qbs[oq]
            bi, bank, war = pool.take()

            def qmm(e, g=g, qb=qb, bank=bank):
                ins = None
                for ic in range(8):
                    ins = e.matmul(bank[:, 0:512],
                                   wbuf[:, ic * 1024 + g * 128: ic * 1024 + g * 128 + 128],
                                   lnxT[:, ic * S + qb * 512: ic * S + (qb + 1) * 512],
                                   start=(ic == 0), stop=(ic == 7))
                return ins
            t_mm = P.do("tensor", qmm, deps=[t_wq, war] + t_lnxT_all)
            t_ev2 = P.do("vector", lambda e, g=g, oq=oq, bank=bank: e.tensor_scalar(
                qTo[:, g * HALF + oq * 512: g * HALF + (oq + 1) * 512],
                bank[:, 0:512], bq_sb[:, g:g + 1], None, ALU.add),
                deps=[t_mm, t_small], hard=True)
            pool.set_read(bi, t_ev2)
            t_q.append(t_ev2)

    t_wo = P.do("gpsimd", wload(woT), dma=True, n=8, deps=[t_q[-1]])

    t_kin_free = [None, None]
    t_vin_free = [None, None]
    t_pb_free = [None, None]
    t_zploads = []
    t_ctx_ev = []
    zq_banks = {}
    kv_par = 0
    pb_par = 0
    for oq in range(4):
        qb = own_qbs[oq]
        n_kt = qb * 4 + 4
        bi_z, bz, war_z = pool.take()
        t_z_mm_all = []
        for g in range(8):
            t_qk_all = []
            t_pv_last = None
            ktgroups = [(s0, min(s0 + 4, n_kt)) for s0 in range(0, n_kt, 4)]
            for (k0, k1) in ktgroups:
                pk = kv_par % 2
                kv_par += 1
                t_kld = P.do("sync", lambda e, g=g, k0=k0, k1=k1, pk=pk: e.dma_start(
                    out=kbuf[pk][:, 0:(k1 - k0) * 128],
                    in_=kscr[0:128, g * S + k0 * 128: g * S + k1 * 128]), dma=True,
                    deps=[t_kin_free[pk]] + t_kout)
                t_vld = P.do("sync", lambda e, g=g, k0=k0, k1=k1, pk=pk: e.dma_start(
                    out=vbuf[pk][:, 0:(k1 - k0) * 130].rearrange("p (t c) -> p t c", c=130),
                    in_=vscr[0:128, :].rearrange("p (t c) -> p t c", c=1040)
                        [:, k0:k1, 2 * g * 65: 2 * g * 65 + 130]), dma=True,
                    deps=[t_vin_free[pk]] + t_vout)
                for kt in range(k0, k1):
                    kl = kt - k0
                    ppb = pb_par % 2
                    pb_par += 1

                    def qkmm(e, g=g, qb=qb, oq=oq, kt=kt, kl=kl, pk=pk):
                        e.matmul(SP[:, 0:512], kbuf[pk][0:64, kl * 128:(kl + 1) * 128],
                                 qTo[0:64, g * HALF + oq * 512: g * HALF + (oq + 1) * 512],
                                 start=True, stop=True, tile_position=(0, 0))
                        return e.matmul(SP[:, 512:1024],
                                        kbuf[pk][64:128, kl * 128:(kl + 1) * 128],
                                        qTo[64:128, g * HALF + oq * 512: g * HALF + (oq + 1) * 512],
                                        start=True, stop=True, tile_position=(64, 0))
                    t_qk = P.do("tensor", qkmm,
                                deps=[t_kld, t_q[g * 4 + oq],
                                      t_qk_all[-1] if t_qk_all else None])
                    o_rel = kt - qb * 4
                    if 0 <= o_rel <= 3:
                        mop = ALU.mult if mode == "l0" else ALU.add

                        def mfn(e, o_rel=o_rel, mop=mop):
                            return e.tensor_tensor(SP[:, 0:1024], SP[:, 0:1024],
                                                   mask_sb[:, o_rel * 1024:(o_rel + 1) * 1024],
                                                   mop)
                        t_mask = P.do("vector", mfn, deps=[t_qk, t_small], hard=True)
                    else:
                        t_mask = t_qk
                    t_exp = P.do("scalar", lambda e, ppb=ppb: e.activation(
                        pbuf[ppb][:, :], SP[:, 0:1024], AF.Exp, scale=1.0 / 8.0,
                        bias=bias_m8[:, 0:1]), deps=[t_mask, t_const, t_pb_free[ppb]],
                        hard=True)
                    t_qk_all.append(t_exp)

                    def pvmm(e, kt=kt, kl=kl, pk=pk, ppb=ppb, first=(kt == 0),
                            last=(kt == n_kt - 1 and mode == "l1")):
                        e.matmul(ctxA[0:65, 0:512], vbuf[pk][:, kl * 130: kl * 130 + 65],
                                 pbuf[ppb][:, 0:512], start=first, stop=False)
                        return e.matmul(ctxB[0:65, 0:512],
                                        vbuf[pk][:, kl * 130 + 65: kl * 130 + 130],
                                        pbuf[ppb][:, 512:1024], start=first, stop=False)
                    t_pv = P.do("tensor", pvmm,
                                deps=[t_exp, t_vld,
                                      (t_ctx_ev[-1] if t_ctx_ev else None) if kt == 0 else None])
                    t_pv_last = t_pv
                    t_pb_free[ppb] = t_pv
                t_kin_free[pk] = t_qk_all[-1] if t_qk_all else None
                t_vin_free[pk] = t_pv_last
            if mode == "l0":
                for k0 in range(n_kt, 32, 4):
                    k1 = min(k0 + 4, 32)
                    pk = kv_par % 2
                    kv_par += 1
                    t_vld = P.do("sync", lambda e, g=g, k0=k0, k1=k1, pk=pk: e.dma_start(
                        out=vbuf[pk][:, 0:(k1 - k0) * 130].rearrange("p (t c) -> p t c", c=130),
                        in_=vscr[0:128, :].rearrange("p (t c) -> p t c", c=1040)
                            [:, k0:k1, 2 * g * 65: 2 * g * 65 + 130]), dma=True,
                        deps=[t_vin_free[pk]] + t_vout)
                    t_pv2 = None
                    for kt in range(k0, k1):
                        kl = kt - k0

                        def pvc(e, kl=kl, pk=pk):
                            e.matmul(ctxA[0:65, 0:512], vbuf[pk][:, kl * 130: kl * 130 + 65],
                                     pconst[:, 0:512], start=False, stop=False)
                            return e.matmul(ctxB[0:65, 0:512],
                                            vbuf[pk][:, kl * 130 + 65: kl * 130 + 130],
                                            pconst[:, 512:1024], start=False, stop=False)
                        t_pv2 = P.do("tensor", pvc, deps=[t_vld, t_const])
                    t_vin_free[pk] = t_pv2
                    t_pv_last = t_pv2
            def fin(e):
                e.matmul(ctxA[0:65, 0:512], ones[0:1, 0:65], zrow_zero[0:1, 0:512],
                         start=False, stop=True)
                return e.matmul(ctxB[0:65, 0:512], ones[0:1, 0:65], zrow_zero[0:1, 0:512],
                                start=False, stop=True)
            t_fin = P.do("tensor", fin, deps=[t_pv_last, t_const])

            def ctxev(e, g=g, oq=oq):
                e.tensor_copy(ctxT[0:64, g * HALF + oq * 512: g * HALF + (oq + 1) * 512],
                              ctxA[0:64, 0:512])
                e.tensor_copy(ctxT[64:128, g * HALF + oq * 512: g * HALF + (oq + 1) * 512],
                              ctxB[0:64, 0:512])
                e.tensor_copy(zrowAB[0:1, 0:512], ctxA[64:65, 0:512])
                return e.tensor_copy(zrowAB[0:1, 512:1024], ctxB[64:65, 0:512])
            t_cev = P.do("vector", ctxev, deps=[t_fin, t_ctx_ev[-1] if t_ctx_ev else None],
                         hard=True)
            t_ctx_ev.append(t_cev)

            def zt(e, g=g, bz=bz):
                ins = None
                for ab in range(2):
                    for qt in range(4):
                        col = qt * 16 + 2 * g + ab
                        ins = e.matmul(bz[:, col:col + 1],
                                       zrowAB[0:1, ab * 512 + qt * 128: ab * 512 + (qt + 1) * 128],
                                       one1[0:1, 0:1], start=True, stop=True)
                return ins
            t_zt = P.do("tensor", zt, deps=[t_cev, war_z if g == 0 else None, t_const])
            t_z_mm_all.append(t_zt)
        t_zr = P.do("vector", lambda e, oq=oq, bz=bz: e.reciprocal(
            zrec[:, oq * 64:(oq + 1) * 64], bz[:, 0:64]), deps=t_z_mm_all, hard=True)
        pool.set_read(bi_z, t_zr)
        zq_banks[oq] = t_zr

    ctxnT = qTo
    t_attn_done = [t_ctx_ev[-1]] + list(zq_banks.values())
    prev_tr2 = None
    prev_ev2 = None
    prev_norm2 = None
    t_ctxn = [None] * 16
    for oqt in range(16):
        oq, qt = oqt // 4, oqt % 4

        def tr1(e, oqt=oqt):
            ins = None
            for g in range(8):
                ins = e.transpose(trans_ps[:, g * 128:(g + 1) * 128],
                                  ctxT[:, g * HALF + oqt * 128:(g * HALF + (oqt + 1) * 128)],
                                  id_sb[:, :])
            return ins
        t_t1 = P.do("tensor", tr1, deps=t_attn_done + [prev_ev2, t_small])

        def ev1(e):
            return e.tensor_copy(ctx_tm[:, :], trans_ps[:, :])
        t_e1 = P.do("vector", ev1, deps=[t_t1, prev_norm2], hard=True)

        def nrm(e, oq=oq, qt=qt):
            ins = None
            for h in range(16):
                ins = e.tensor_scalar(ctx_tm[:, h * 64:(h + 1) * 64],
                                      ctx_tm[:, h * 64:(h + 1) * 64],
                                      zrec[:, oq * 64 + qt * 16 + h: oq * 64 + qt * 16 + h + 1],
                                      None, ALU.mult)
            return ins
        t_n = P.do("vector", nrm, deps=[t_e1, zq_banks[oq]], hard=True)
        prev_norm2 = t_n

        def tr2(e, oqt=oqt):
            ins = None
            for g in range(8):
                ins = e.transpose(trans_ps[:, g * 128:(g + 1) * 128],
                                  ctx_tm[:, g * 128:(g + 1) * 128], id_sb[:, :])
            return ins
        t_t2 = P.do("tensor", tr2, deps=[t_n, prev_ev2])

        def ev2(e, oqt=oqt):
            dst = ctxnT[:, :].rearrange("q (g r) -> q g r", g=8)[:, :, oqt * 128:(oqt + 1) * 128]
            srcv = trans_ps[:, :].rearrange("q (g r) -> q g r", g=8)
            return e.tensor_copy(dst, srcv)
        t_e2 = P.do("vector", ev2, deps=[t_t2], hard=True)
        prev_ev2 = t_e2
        t_ctxn[oqt] = t_e2

    prev_y = None
    prev_xl2 = prev_norm
    prev_outev = None
    for oqt in range(16):
        i_glob = own_tiles[oqt]
        ev_toks = []
        for half in range(2):
            bi, bank, war = pool.take()

            def ofn(e, oqt=oqt, half=half, bank=bank):
                for g in range(8):
                    e.matmul(bank[:, 0:512],
                             ctxnT[:, g * HALF + oqt * 128: g * HALF + (oqt + 1) * 128],
                             wbuf[:, g * 1024 + half * 512: g * 1024 + half * 512 + 512],
                             start=(g == 0), stop=False)
                return e.matmul(bank[:, 0:512], ones[0:1, 0:128],
                                bo_sb[0:1, half * 512: half * 512 + 512],
                                start=False, stop=True)
            t_o = P.do("tensor", ofn, deps=[t_wo, t_ctxn[oqt], war, t_small, t_const])
            t_ev3 = P.do("vector", lambda e, half=half, bank=bank: e.tensor_copy(
                out_tm[:, half * 512:(half + 1) * 512], bank[:, 0:512]),
                deps=[t_o, prev_outev])
            pool.set_read(bi, t_ev3)
            ev_toks.append(t_ev3)
        t_normo = emit_ln_tile(P, out_tm[:, :], norm[:, :], tmp, deps=ev_toks + [prev_y])
        prev_outev = t_normo
        t_xl = P.do("sync", lambda e, i_glob=i_glob: e.dma_start(
            out=xst[:, :], in_=xin[i_glob * 128:(i_glob + 1) * 128, :]), dma=True,
            deps=[prev_xl2])
        t_y = P.do("vector", lambda e: e.tensor_tensor(ybuf[:, :], norm[:, :], xst[:, :],
                                                       ALU.add), deps=[t_normo, t_xl])
        prev_xl2 = t_y
        t_st = P.do("gpsimd", lambda e, oqt=oqt: e.dma_start(
            out=yout[oqt * 128:(oqt + 1) * 128, :], in_=ybuf[:, :]), dma=True, deps=[t_y])
        prev_y = t_st

    P.emit()
    es.close()
    return nc


def make_attn_masks(mode):
    m = np.zeros((128, 4 * 1024), np.float32)
    for o in range(4):
        p = np.arange(128)[:, None]
        c = np.arange(512)[None, :]
        if mode == "l0":
            allow = (o * 4 + p // 32) <= (c // 32)
            tile = np.where(allow, 1.0, 0.0)
        else:
            allow = (o * 128 + p) <= c
            tile = np.where(allow, 0.0, -1e9)
        m[:, o * 1024: o * 1024 + 512] = tile
        m[:, o * 1024 + 512: o * 1024 + 1024] = tile
    return m.astype(bfdt)


def prep_attn_inputs(inp, x, mode):
    f32 = np.float32
    T = lambda a: np.ascontiguousarray(np.asarray(a, f32).T)
    C = lambda a: np.ascontiguousarray(np.asarray(a, f32))
    if mode == "l0":
        wq, wk, wv = T(inp["L0_attn_Wq"]), T(inp["L0_attn_Wk"]), T(inp["L0_attn_Wv"])
        wo = T(inp["L0_attn_Wo"])
        bq, bk = np.asarray(inp["L0_attn_bq"], f32), np.asarray(inp["L0_attn_bk"], f32)
        bv, bo = np.asarray(inp["L0_attn_bv"], f32), np.asarray(inp["L0_attn_bo"], f32)
    else:
        wqkv = np.asarray(inp["L1_attn_w_qkv"], f32)
        wq, wk, wv = C(wqkv[:, 0:D]), C(wqkv[:, D:2 * D]), C(wqkv[:, 2 * D:3 * D])
        wo = C(inp["L1_attn_w_proj"])
        bqkv = np.asarray(inp["L1_attn_b_qkv"], f32)
        bq, bk, bv = bqkv[0:D], bqkv[D:2 * D], bqkv[2 * D:3 * D]
        bo = np.asarray(inp["L1_attn_b_proj"], f32)
    base = {
        "wqT": wq, "wkT": wk, "wvT": wv, "woT": wo,
        "bqf": np.ascontiguousarray(bq.reshape(8, 128).T),
        "bkf": np.ascontiguousarray(bk.reshape(8, 128).T),
        "bvr": bv.reshape(1, D), "bor": bo.reshape(1, D),
        "maskc": make_attn_masks(mode),
        "ident": np.eye(128, dtype=bfdt),
    }
    maps = []
    for c in range(NCORE):
        b = c >> 1
        m = dict(base)
        m["xin"] = np.ascontiguousarray(x[b].astype(f32))
        maps.append(m)
    return maps


def run_attn(inp, x, mode):
    nc0 = build_attn(mode, 0)
    nc1 = build_attn(mode, 1)
    maps = prep_attn_inputs(inp, x, mode)
    res0 = run_bass_kernel_spmd(nc0, [maps[c] for c in range(0, NCORE, 2)],
                                core_ids=[0, 1, 2, 3])
    res1 = run_bass_kernel_spmd(nc1, [maps[c] for c in range(1, NCORE, 2)],
                                core_ids=[0, 1, 2, 3])
    y = np.empty((B, S, D), np.float32)
    for b in range(B):
        o0 = res0.results[b]["y"]
        o1 = res1.results[b]["y"]
        y[b, 0:1024] = o0[0:1024]
        y[b, 3072:4096] = o0[1024:2048]
        y[b, 1024:3072] = o1
    return y


def prep_l0ffn_inputs(inp, x1):
    f32 = np.float32
    g = lambda n: np.asarray(inp["L0_ffn_" + n], f32)
    wd2 = g("Wd2")
    bd2 = g("bd2")
    w2a = np.concatenate([wd2, bd2[:, None, :]], axis=1)
    w2aT = np.ascontiguousarray(np.transpose(w2a, (0, 2, 1)))
    wua = np.concatenate([g("Wa_up"), g("ba_up")[:, None, :]], axis=1)
    base = {
        "wbase": g("W_base"),
        "bbase": g("b_base").reshape(1, D),
        "wd1": g("Wd1"),
        "bd1r": g("bd1").reshape(1, 8 * 256),
        "w2aT": w2aT,
        "wadr": g("Wa_down"),
        "badT": np.ascontiguousarray(g("ba_down").T),
        "wua": np.ascontiguousarray(wua),
        "ident": np.eye(128, dtype=bfdt),
    }
    maps = []
    for c in range(NCORE):
        b = c >> 1
        m = dict(base)
        m["x1"] = np.ascontiguousarray(x1[b].astype(f32))
        maps.append(m)
    return maps


def assemble_half_outputs(res):
    y = np.empty((B, S, D), np.float32)
    for c in range(NCORE):
        b, j = c >> 1, c & 1
        o = res.results[c]["y"]
        if j == 0:
            y[b, 0:1024] = o[0:1024]
            y[b, 3072:4096] = o[1024:2048]
        else:
            y[b, 1024:3072] = o
    return y


def run_l0ffn(inp, x1):
    nc0 = build_l0ffn_j(0)
    nc1 = build_l0ffn_j(1)
    maps = prep_l0ffn_inputs(inp, x1)
    res0 = run_bass_kernel_spmd(nc0, [maps[c] for c in range(0, NCORE, 2)],
                                core_ids=[0, 1, 2, 3])
    res1 = run_bass_kernel_spmd(nc1, [maps[c] for c in range(1, NCORE, 2)],
                                core_ids=[0, 1, 2, 3])
    y = np.empty((B, S, D), np.float32)
    for b in range(B):
        o0 = res0.results[b]["y"]
        o1 = res1.results[b]["y"]
        y[b, 0:1024] = o0[0:1024]
        y[b, 3072:4096] = o0[1024:2048]
        y[b, 1024:3072] = o1
    return y


def prep_l1ffn_inputs(inp, x3):
    T = lambda a: np.ascontiguousarray(np.asarray(a).T.astype(np.float32))
    base = {
        "wfT": T(inp["L1_ffn_wf"]), "wiT": T(inp["L1_ffn_wi"]),
        "wcT": T(inp["L1_ffn_wc"]), "woT": T(inp["L1_ffn_wo"]),
        "ufT": T(inp["L1_ffn_uf"]), "uiT": T(inp["L1_ffn_ui"]),
        "ucT": T(inp["L1_ffn_uc"]), "uoT": T(inp["L1_ffn_uo"]),
        "bfm": np.ascontiguousarray(np.stack(
            [np.asarray(inp["L1_ffn_b" + n]).reshape(8, 128).T for n in ("f", "i", "c")],
            axis=1).reshape(128, 24).astype(np.float32)),
        "bo_row": np.asarray(inp["L1_ffn_bo"]).reshape(1, D).astype(np.float32),
        "inc_fm": np.ascontiguousarray(np.asarray(inp["L1_ffn_in_c"]).reshape(8, 128).T.astype(np.float32)),
        "ident": np.eye(128, dtype=bfdt),
    }
    maps = []
    for c in range(NCORE):
        m = dict(base)
        m["x4"] = np.ascontiguousarray(x3[:, c * PCHUNK:(c + 1) * PCHUNK, :].astype(np.float32))
        maps.append(m)
    return maps


def run_l1ffn(inp, x3):
    nc = build_l1ffn()
    maps = prep_l1ffn_inputs(inp, x3)
    res = run_bass_kernel_spmd(nc, maps, core_ids=list(range(NCORE)))
    y = np.empty((B, S, D), np.float32)
    for c in range(NCORE):
        y[:, c * PCHUNK:(c + 1) * PCHUNK, :] = res.results[c]["y"]
    return y


def kernel(**inputs):
    x0 = np.asarray(inputs["L0_attn_hidden_states"], np.float32)
    x1 = run_attn(inputs, x0, "l0")
    x2 = run_l0ffn(inputs, x1)
    x3 = run_attn(inputs, x2, "l1")
    y = run_l1ffn(inputs, x3)
    return y
